# version 01
# baseline (speedup 1.0000x reference)
import contextlib
import numpy as np
import ml_dtypes
import concourse.bass as bass
import concourse.mybir as mybir
from concourse.bass_utils import run_bass_kernel_spmd

F32 = mybir.dt.float32
BF16 = mybir.dt.bfloat16
I8 = mybir.dt.int8
U8 = mybir.dt.uint8
AF = mybir.ActivationFunctionType
ALU = mybir.AluOpType

D = 2048
DT = 16
S = 16
TOUT = 1024
PAD = 64
T = PAD + TOUT
NO = 8
NG = 6
INTER = 8192
IT = 64
WIN3 = [(0, 512), (512, 1024), (1024, T)]
EPS = 1e-5
SBUF_BYTES = 212832

ENGINES = ("sync", "gpsimd", "scalar", "vector", "tensor")


class Trk:
    def __init__(self, name, step=1):
        self.name, self.step, self.n, self.h = name, step, 0, None


class Alloc:
    def __init__(self, nbytes):
        self.cap = nbytes
        self.items = []

    def alloc(self, size, t0, t1, name):
        size = (size + 63) & ~63
        spans = sorted((o, o + s) for (o, s, a0, a1, _n) in self.items
                       if not (a1 <= t0 or t1 <= a0))
        off = 0
        for (s0, s1) in spans:
            if off + size <= s0:
                break
            off = max(off, s1)
        if off + size > self.cap:
            raise MemoryError(f"SBUF overflow: {name} {size}B t[{t0},{t1}] off {off}")
        self.items.append((off, size, t0, t1, name))
        return off


def build_program(taps=()):
    nc = bass.Bass("TRN2", target_bir_lowering=False, debug=False)

    d_x = nc.dram_tensor("xT", [128, DT * T], F32, kind="ExternalInput").ap()
    d_win = nc.dram_tensor("w_in_t", [DT, 128, 2048], F32, kind="ExternalInput").ap()
    d_wout = nc.dram_tensor("w_out_t", [DT, 128, 2048], F32, kind="ExternalInput").ap()
    d_bwt = nc.dram_tensor("b_w_t", [128, 256], F32, kind="ExternalInput").ap()
    d_cwt = nc.dram_tensor("c_w_t", [16, 2048], F32, kind="ExternalInput").ap()
    d_a = nc.dram_tensor("a_t", [128, 256], F32, kind="ExternalInput").ap()
    d_dcol = nc.dram_tensor("dcol", [128, 16], F32, kind="ExternalInput").ap()
    d_bin = nc.dram_tensor("bincol", [128, 16], F32, kind="ExternalInput").ap()
    d_bout = nc.dram_tensor("boutcol", [128, 16], F32, kind="ExternalInput").ap()
    d_sd = nc.dram_tensor("sdcol", [128, 16], F32, kind="ExternalInput").ap()
    d_sg = nc.dram_tensor("sgcol", [128, 64], F32, kind="ExternalInput").ap()
    d_su = nc.dram_tensor("sucol", [128, 64], F32, kind="ExternalInput").ap()
    d_wg = nc.dram_tensor("wg_t", [IT, 128, 2048], I8, kind="ExternalInput").ap()
    d_wu = nc.dram_tensor("wu_t", [IT, 128, 2048], I8, kind="ExternalInput").ap()
    d_wd = nc.dram_tensor("wd_t", [DT, 2, 128, 4096], I8, kind="ExternalInput").ap()
    d_mask = nc.dram_tensor("maskb", [16, T], BF16, kind="ExternalInput").ap()
    d_rep16 = nc.dram_tensor("rep16b", [16, 128], BF16, kind="ExternalInput").ap()
    d_ones_b = nc.dram_tensor("ones_b", [128, 1], BF16, kind="ExternalInput").ap()
    d_ones_f = nc.dram_tensor("ones_f", [1, 128], F32, kind="ExternalInput").ap()
    d_oodf = nc.dram_tensor("ood_f", [1, 128], F32, kind="ExternalInput").ap()
    d_identf = nc.dram_tensor("ident_f", [128, 128], F32, kind="ExternalInput").ap()
    d_out = nc.dram_tensor("out_xT", [128, DT * TOUT], F32, kind="ExternalOutput").ap()
    tap_out = {}
    for tname, tshape, tdt in taps:
        tap_out[tname] = nc.dram_tensor("tap_" + tname, tshape, tdt,
                                        kind="ExternalOutput").ap()

    al = Alloc(SBUF_BYTES)
    o_x = al.alloc(DT * T * 4, 0, 11, "x")
    o_srow = al.alloc(2 * T * 4, 0, 11, "sumrows")
    o_identf = al.alloc(128 * 4, 0, 11, "identf")
    o_scr = al.alloc(1088 * 4, 0, 11, "scratch")
    o_sq = al.alloc(4 * 512 * 2, 0, 11, "sqscr")
    o_const = al.alloc(4 * 64 + 2 * 256 + 256 + 64, 0, 11, "consts")
    o_onesf = al.alloc(2 * 128 * 4, 0, 11, "onesf+oodf")
    o_dly = al.alloc(512 * 2, 0, 11, "delay")
    o_bcb = al.alloc(2 * 2048 + 3 * 1024, 5, 11, "bcastB")
    o_y2a = al.alloc(DT * 512 * 2, 7, 11, "y2b_w0")
    o_y2b = al.alloc(DT * 512 * 2, 9, 11, "y2b_w1")
    o_xn1 = al.alloc(DT * T * 2, 0, 6, "xn1b")
    o_xpb = al.alloc(DT * T * 2, 1, 6, "x_projb/y_attnb")
    o_pk = al.alloc(DT * NG * 128 * 2, 0, 4, "packs")
    o_gh = al.alloc(NG * 576 * 4, 3, 4, "ghat")
    o_dr = al.alloc(T * 4, 2, 4, "deltar")
    o_db = al.alloc(T * 2, 2, 3, "deltab")
    o_mask = al.alloc(T * 2, 0, 3, "maskb")
    o_acc1 = al.alloc(520 * 4, 3, 4, "acc1")
    o_sh = al.alloc(TOUT * 4, 3, 4, "S_h")
    o_shb = al.alloc(TOUT * 2, 3, 5, "S_hb")
    o_bca = al.alloc(2048 + 1024, 0, 2, "bcastA")
    o_wri = al.alloc(2 * 4096, 0, 2, "wring_in")
    o_wro = al.alloc(2 * 4096, 5, 6, "wring_out")
    o_wrf = al.alloc(4 * 4096, 6, 10, "wring_ffn")
    o_bwt = al.alloc(256 * 2, 0, 3, "bwtb")
    o_cwt = al.alloc(2048 * 2, 4, 5, "cwtb")
    o_sig = al.alloc(2 * 1024, 0, 1, "sig")
    o_pkf = al.alloc(2 * 8192, 0, 1, "pkf")
    o_ypre = [al.alloc(TOUT * 2, 4, 6, f"ypre{t}") for t in range(DT)]
    o_x2na = al.alloc(DT * 512 * 2, 6, 7, "x2nb_w0")
    o_x2nb = al.alloc(DT * 512 * 2, 8, 9, "x2nb_w1")
    o_fuse = [al.alloc(512 * 2, 6, 10, f"fuse{k}") for k in range(IT)]
    import os as _os
    if _os.environ.get("KDEBUG_ALLOC"):
        for (o, s, a0, a1, n) in sorted(al.items):
            print(f"{o:7d} +{s:6d} t[{a0:2d},{a1:2d}] {n}")
        raise SystemExit(0)

    plan = {e: [] for e in ENGINES}

    def Praw(eng, fn):
        plan[eng].append(fn)

    t_pe, t_vec, t_scl = Trk("t_pe"), Trk("t_vec"), Trk("t_scl")
    s_const = Trk("s_const", 16)
    s_x = [Trk(f"s_x{i}", 16) for i in range(3)]
    s_bwt, s_cwt = Trk("s_bwt", 16), Trk("s_cwt", 16)
    s_wri = [Trk(f"s_wri{i}", 16) for i in range(2)]
    s_wro = [Trk(f"s_wro{i}", 16) for i in range(2)]
    s_wrf = [Trk(f"s_wrf{i}", 16) for i in range(4)]
    s_wrd = [Trk(f"s_wrd{i}", 16) for i in range(2)]
    s_out, s_tap = Trk("s_out", 16), Trk("s_tap", 16)
    DMAT = ([s_const] + s_x + [s_bwt, s_cwt] + s_wri + s_wro + s_wrf + s_wrd
            + [s_out, s_tap])
    TRKS = [t_pe, t_vec, t_scl] + DMAT
    ETRK = {"tensor": t_pe, "vector": t_vec, "scalar": t_scl}

    def DMA(queue, trk, build):
        trk.n += trk.step
        plan[queue].append(lambda e, b=build, t=trk: b(e).then_inc(t.h, t.step))
        return (trk, trk.n)

    def W(eng, trk, val=None):
        if val is None:
            trk, val = trk
        if val and val > 0:
            Praw(eng, lambda e, t=trk, v=val: e.wait_ge(t.h, v))

    def P(eng, fn):
        trk = ETRK[eng]
        trk.n += trk.step
        plan[eng].append(lambda e, b=fn, t=trk: b(e).then_inc(t.h, t.step))
        return trk.n

    def E(eng, build, inc=False):
        if inc:
            return P(eng, build)
        Praw(eng, build)
        return None

    def M(eng):
        return ETRK[eng].n

    def CB(eng):
        trk = ETRK[eng]
        Praw(eng, lambda e, t=trk, v=trk.n: e.wait_ge(t.h, v))

    def BAR():
        for eng in ("tensor", "vector", "scalar"):
            for trk in (t_pe, t_vec, t_scl):
                if trk.n > 0:
                    Praw(eng, lambda e, t=trk, v=trk.n: e.wait_ge(t.h, v))

    class Bank:
        def __init__(self, idx):
            self.idx, self.w_pe, self.r_trk = idx, 0, None

    banks = [Bank(i) for i in range(8)]

    def pe_acquire(bk):
        if bk.r_trk is not None:
            W("tensor", bk.r_trk[0], bk.r_trk[1])

    def mm(out_ap, lhsT, rhs, start, stop, inc=True):
        b = (lambda e, o=out_ap, l=lhsT, r=rhs, st=start, sp=stop:
             e.matmul(o, l, r, start=st, stop=sp))
        t_pe.n += 1
        plan["tensor"].append(lambda e, bb=b: bb(e).then_inc(t_pe.h, 1))
        return t_pe.n

    def act_table(set_name):
        from concourse.hw_specs import get_activation_tables
        sid = list(get_activation_tables(nc.m.arch).keys()).index(set_name)

        Praw("scalar", lambda e, sid=sid: e.add_instruction(
            mybir.InstLoadActFuncSet(
                name=nc.get_next_instruction_name(), act_func_set_id=sid,
                ins=[], outs=[])))
        for _ in range(12):
            P("scalar", lambda e: e.copy(out=DLY, in_=DLY))
            CB("scalar")

    es = contextlib.ExitStack()
    nc._kernel_es = es
    ar = es.enter_context(nc.sbuf_tensor("arena", [128, SBUF_BYTES], U8))
    pb = [es.enter_context(nc.psum_tensor(f"pb{i}", [128, 512], F32))
          for i in range(8)]
    for trk in TRKS:
        trk.h = es.enter_context(nc.semaphore(trk.name))

    def v_f32(off, part, cols):
        return ar[0:part, off:off + cols * 4].bitcast(F32)

    def v_bf(off, part, cols):
        return ar[0:part, off:off + cols * 2].bitcast(BF16)

    def Xj(j, c0, c1):
        return v_f32(o_x + (j * T + c0) * 4, 128, c1 - c0)

    def XN1j(j, c0, c1):
        return v_bf(o_xn1 + (j * T + c0) * 2, 128, c1 - c0)

    def XPBj(j, c0, c1):
        return v_bf(o_xpb + (j * T + c0) * 2, 128, c1 - c0)

    def YATT(j, l0, l1):
        return v_bf(o_xpb + (j * T + l0) * 2, 128, l1 - l0)

    def PACK(j, g):
        return v_bf(o_pk + (j * NG + g) * 256, 128, 128)

    def GH(g, c0, c1):
        return v_f32(o_gh + (g * 576 + c0) * 4, 128, c1 - c0)

    DR = v_f32(o_dr, 128, T)
    DB = v_bf(o_db, 16, T)
    MASK = v_bf(o_mask, 16, T)
    ACC1 = v_f32(o_acc1, 128, 520)
    SH = v_f32(o_sh, 16, TOUT)
    SHB = v_bf(o_shb, 16, TOUT)
    SCR = v_f32(o_scr, 128, 1088)
    SRA = v_f32(o_srow, 1, T)
    SRB = v_f32(o_srow + T * 4, 1, T)
    CHS = v_f32(o_scr, 128, 512)
    CH2 = v_f32(o_scr + 544 * 4, 128, 512)

    def SQT(i):
        return v_bf(o_sq + i * 1024, 128, 512)

    MB1F = v_f32(o_bca, 128, 512)
    INV1B = v_bf(o_bca + 2048, 128, 512)
    MB_F = v_f32(o_bcb, 128, 512)
    MB_F2 = v_f32(o_bcb + 2048, 128, 512)
    INVB = v_bf(o_bcb + 4096, 128, 512)
    INVB2 = v_bf(o_bcb + 5120, 128, 512)
    MB_B = v_bf(o_bcb + 6144, 128, 512)

    def WRI(i):
        return v_bf(o_wri + i * 4096, 128, 2048)

    def WRO(i):
        return v_bf(o_wro + i * 4096, 128, 2048)

    def WRS(i):
        return v_bf(o_wrf + i * 4096, 128, 2048)

    def WRD(i):
        return v_bf(o_wrf + i * 8192, 128, 4096)

    BWTB = v_bf(o_bwt, 128, 256)
    CWTB = v_bf(o_cwt, 16, 2048)
    oc = o_const
    DCOL = v_f32(oc, 128, 16); oc += 64
    BINC = v_f32(oc, 128, 16); oc += 64
    BOUTC = v_f32(oc, 128, 16); oc += 64
    SDC = v_f32(oc, 128, 16); oc += 64
    SGC = v_f32(oc, 128, 64); oc += 256
    SUC = v_f32(oc, 128, 64); oc += 256
    REP16 = v_bf(oc, 16, 128); oc += 256
    ONESB = v_bf(oc, 128, 1); oc += 64
    ONESF = v_f32(o_onesf, 1, 128)
    OODF = v_f32(o_onesf + 512, 1, 128)
    IDENTF = v_f32(o_identf, 128, 128)
    DLY = v_bf(o_dly, 128, 512)
    SIGA = v_f32(o_sig, 128, 256)
    SIG8 = v_f32(o_sig + 1024, 128, 256)
    PKF = v_f32(o_pkf, 128, 2048)
    S8R = v_f32(o_pkf + 8192, 128, 2048)

    def YPRE(t, l0, l1):
        return v_bf(o_ypre[t] + l0 * 2, 128, l1 - l0)

    def FUSE(k):
        return v_bf(o_fuse[k], 128, 512)

    def X2NW(k, j):
        return v_bf((o_x2na if k == 0 else o_x2nb) + j * 1024, 128, 512)

    def Y2W(k, j):
        return v_bf((o_y2a if k == 0 else o_y2b) + j * 1024, 128, 512)

    GS = v_f32(o_scr, 128, 512)

    for (dst, src) in [(SIGA, d_a), (DCOL, d_dcol), (BINC, d_bin),
                       (BOUTC, d_bout), (SDC, d_sd), (SGC, d_sg), (SUC, d_su),
                       (MASK, d_mask), (REP16, d_rep16), (ONESB, d_ones_b),
                       (ONESF, d_ones_f), (OODF, d_oodf), (IDENTF, d_identf)]:
        DMA("sync", s_const, lambda e, d=dst, s=src: e.dma_start(out=d, in_=s))
    v_consts = (s_const, s_const.n)
    d_x3 = d_x.rearrange("p (j c) -> p j c", j=DT)
    x_ar3 = ar[:, o_x:o_x + DT * T * 4].bitcast(F32).rearrange(
        "p (j c) -> p j c", j=DT)
    v_xw = []
    for wi, (w0, w1) in enumerate(WIN3):
        v_xw.append(DMA("sync", s_x[wi],
                        lambda e, a=x_ar3[:, :, w0:w1], s=d_x3[:, :, w0:w1]:
                        e.dma_start(out=a, in_=s)))
    v_bwt = DMA("gpsimd", s_bwt, lambda e: e.dma_start(out=BWTB, in_=d_bwt))
    v_win = {}
    for t in range(2):
        v_win[t] = DMA("gpsimd", s_wri[t % 2],
                       lambda e, t=t: e.dma_start(out=WRI(t % 2), in_=d_win[t]))

    P("vector", lambda e: e.memset(DLY, 0.0))
    v_dly = M("vector")
    W("scalar", t_vec, v_dly)
    act_table("natural_log_exp_and_others")

    W("vector", v_consts)
    W("scalar", v_consts)
    P("scalar", lambda e: e.activation(out=SCR[:, 0:256], in_=SIGA, func=AF.Exp,
                                       scale=-1.0))
    v_expA = M("scalar")
    W("vector", t_scl, v_expA)
    P("vector", lambda e: e.tensor_scalar_add(out=SCR[:, 0:256],
                                              in0=SCR[:, 0:256], scalar1=1.0))
    CB("vector")
    P("vector", lambda e: e.reciprocal(out=SIGA, in_=SCR[:, 0:256]))
    CB("vector")
    P("vector", lambda e: e.tensor_mul(out=SCR[:, 0:256], in0=SIGA, in1=SIGA))
    CB("vector")
    P("vector", lambda e: e.tensor_mul(out=SCR[:, 256:512], in0=SCR[:, 0:256],
                                       in1=SCR[:, 0:256]))
    CB("vector")
    P("vector", lambda e: e.tensor_mul(out=SIG8, in0=SCR[:, 256:512],
                                       in1=SCR[:, 256:512]))
    CB("vector")
    s8r4 = v_f32(o_pkf + 8192, 128, 2048).rearrange("p (j o s) -> p j o s",
                                                    j=16, o=8)
    sig8_3 = SIG8.rearrange("p (j s) -> p j s", j=16)
    for o in range(NO):
        P("vector", lambda e, o=o: e.tensor_copy(out=s8r4[:, :, o, :], in_=sig8_3))
    CB("vector")
    pkf4 = v_f32(o_pkf, 128, 2048).rearrange("p (j o s) -> p j o s", j=16, o=8)
    siga3 = SIGA.rearrange("p (j s) -> p j s", j=16)
    P("vector", lambda e: e.memset(pkf4[:, :, 0, :], 1.0))
    CB("vector")
    for o in range(1, NO):
        P("vector", lambda e, o=o: e.tensor_tensor(
            out=pkf4[:, :, o, :], in0=pkf4[:, :, o - 1, :], in1=siga3,
            op=ALU.mult))
        CB("vector")
    v_pkf0 = M("vector")
    pk5 = ar[:, o_pk:o_pk + DT * NG * 256].bitcast(BF16).rearrange(
        "p (j g c) -> p j g c", j=DT, g=NG)
    pkf3 = v_f32(o_pkf, 128, 2048).rearrange("p (j c) -> p j c", j=16)
    W("scalar", t_vec, v_pkf0)
    for g in range(NG):
        P("scalar", lambda e, g=g: e.copy(out=pk5[:, :, g, :], in_=pkf3))
        if g < NG - 1:
            vv = M("scalar")
            W("vector", t_scl, vv)
            P("vector", lambda e: e.tensor_mul(out=PKF, in0=PKF, in1=S8R))
            CB("vector")
            vv2 = M("vector")
            W("scalar", t_vec, vv2)
    M("scalar")

    ln_state = {"mb_busy": (t_vec, 0)}

    def emit_ln(windows, src_fn, src_is_f32, outs, apply_fn, wait_src):
        for wi, (c0, c1) in enumerate(windows):
            wl = c1 - c0
            pe_acquire(banks[6])
            pe_acquire(banks[7])
            pe_j = {-2: t_pe.n, -1: t_pe.n}
            if wait_src is not None:
                ws = wait_src if isinstance(wait_src, list) else [wait_src]
                for pair in ws:
                    W("scalar", pair)
            for j in range(DT):
                W("scalar", t_pe, pe_j[j - 2])
                if src_is_f32:
                    sl = (j % 2) * 2
                    P("scalar", lambda e, j=j, sl=sl, c0=c0, c1=c1:
                      e.copy(out=SQT(sl)[:, 0:c1 - c0], in_=src_fn(j, c0, c1)))
                    P("scalar", lambda e, j=j, sl=sl, c0=c0, c1=c1:
                      e.activation(out=SQT(sl + 1)[:, 0:c1 - c0],
                                   in_=src_fn(j, c0, c1), func=AF.Square))
                    rhs_sum, rhs_sq = SQT(sl)[:, 0:wl], SQT(sl + 1)[:, 0:wl]
                else:
                    sl = j % 2
                    P("scalar", lambda e, j=j, sl=sl, c0=c0, c1=c1:
                      e.activation(out=SQT(sl)[:, 0:c1 - c0],
                                   in_=src_fn(j, c0, c1), func=AF.Square))
                    rhs_sum, rhs_sq = src_fn(j, c0, c1), SQT(sl)[:, 0:wl]
                vs = M("scalar")
                W("tensor", t_scl, vs)
                mm(pb[6][0:1, 0:wl], ONESB, rhs_sum, j == 0, j == DT - 1)
                pe_j[j] = mm(pb[7][0:1, 0:wl], ONESB, rhs_sq,
                             j == 0, j == DT - 1, inc=True)
            banks[6].w_pe = banks[7].w_pe = t_pe.n
            W("vector", t_pe, t_pe.n)
            P("vector", lambda e, c0=c0, c1=c1, wl=wl: e.tensor_copy(
                out=SRA[0:1, c0:c1], in_=pb[6][0:1, 0:wl]))
            P("vector", lambda e, c0=c0, c1=c1, wl=wl: e.tensor_copy(
                out=SRB[0:1, c0:c1], in_=pb[7][0:1, 0:wl]))
            vv = M("vector")
            banks[6].r_trk = banks[7].r_trk = (t_vec, vv)
        v_rows = t_vec.n
        for wi, (c0, c1) in enumerate(windows):
            wl = c1 - c0
            mbf, mbb, invb = outs[wi]
            W("tensor", t_vec, v_rows)
            pe_acquire(banks[6])
            mm(pb[6][:, 0:wl], OODF, SRA[0:1, c0:c1], True, True, inc=True)
            banks[6].w_pe = t_pe.n
            W("scalar", t_pe, banks[6].w_pe)
            W("scalar", ln_state["mb_busy"][0], ln_state["mb_busy"][1])
            CB("scalar")
            P("scalar", lambda e, wl=wl: e.copy(out=CHS[:, 0:wl],
                                                in_=pb[6][:, 0:wl]))
            if mbf is not None:
                P("scalar", lambda e, d=mbf, wl=wl: e.copy(out=d[:, 0:wl],
                                                           in_=pb[6][:, 0:wl]))
            if mbb is not None:
                P("scalar", lambda e, d=mbb, wl=wl: e.copy(out=d[:, 0:wl],
                                                           in_=pb[6][:, 0:wl]))
            v_m = M("scalar")
            banks[6].r_trk = (t_scl, v_m)
            pe_acquire(banks[7])
            mm(pb[7][:, 0:wl], OODF, SRB[0:1, c0:c1], True, True, inc=True)
            banks[7].w_pe = t_pe.n
            W("vector", t_scl, v_m)
            P("vector", lambda e, wl=wl: e.tensor_mul(
                out=CH2[:, 0:wl], in0=CHS[:, 0:wl], in1=CHS[:, 0:wl]))
            CB("vector")
            W("vector", t_pe, banks[7].w_pe)
            P("vector", lambda e, wl=wl: e.tensor_sub(
                out=CHS[:, 0:wl], in0=pb[7][:, 0:wl], in1=CH2[:, 0:wl]))
            CB("vector")
            P("vector", lambda e, wl=wl: e.tensor_scalar_add(
                out=CHS[:, 0:wl], in0=CHS[:, 0:wl], scalar1=EPS))
            vv2 = M("vector")
            banks[7].r_trk = (t_vec, vv2)
            W("scalar", t_vec, vv2)
            P("scalar", lambda e, wl=wl: e.activation(
                out=CHS[:, 0:wl], in_=CHS[:, 0:wl], func=AF.Ln))
            CB("scalar")
            P("scalar", lambda e, d=invb, wl=wl: e.activation(
                out=d[:, 0:wl], in_=CHS[:, 0:wl], func=AF.Exp, scale=-0.5))
            v_chain = M("scalar")
            if apply_fn is not None:
                apply_fn(wi, c0, c1, v_chain)

    def ln1_apply(wi, c0, c1, v_chain):
        wl = c1 - c0
        W("vector", t_scl, v_chain)
        for j in range(DT):
            P("vector", lambda e, j=j, c0=c0, c1=c1, wl=wl: e.tensor_sub(
                out=SQT(3)[:, 0:wl], in0=Xj(j, c0, c1), in1=MB1F[:, 0:wl]))
            CB("vector")
            P("vector", lambda e, j=j, c0=c0, c1=c1, wl=wl: e.tensor_mul(
                out=XN1j(j, c0, c1), in0=SQT(3)[:, 0:wl], in1=INV1B[:, 0:wl]))
            CB("vector")
        ln_state["mb_busy"] = (t_vec, M("vector"))

    emit_ln(WIN3, lambda j, c0, c1: Xj(j, c0, c1), True,
            [(MB1F, None, INV1B)] * 3, ln1_apply, v_xw)
    v_xn1_done = ln_state["mb_busy"][1]

    BAR()
    W("tensor", t_vec, v_xn1_done)
    for t in range(DT):
        W("tensor", v_win[t])
        for wi, (w0, w1) in enumerate(WIN3):
            wl = w1 - w0
            bk = banks[(t * 3 + wi) % 4]
            pe_acquire(bk)
            for j in range(DT):
                mm(pb[bk.idx][:, 0:wl], WRI(t % 2)[:, j * 128:(j + 1) * 128],
                   XN1j(j, w0, w1), j == 0, j == DT - 1, inc=(j == DT - 1))
            bk.w_pe = t_pe.n
            W("scalar", t_pe, bk.w_pe)
            P("scalar", lambda e, t=t, w0=w0, w1=w1, wl=wl, b=bk.idx:
              e.activation(out=XPBj(t, w0, w1), in_=pb[b][:, 0:wl],
                           func=AF.Identity, bias=BINC[:, t:t + 1]))
            vv = M("scalar")
            bk.r_trk = (t_scl, vv)
        if t + 2 < DT:
            W("gpsimd", t_pe, t_pe.n)
            v_win[t + 2] = DMA("gpsimd", s_wri[t % 2],
                               lambda e, t=t: e.dma_start(out=WRI(t % 2),
                                                          in_=d_win[t + 2]))
    v_xpb_all = t_scl.n

    BAR()
    W("tensor", v_bwt)
    W("tensor", t_scl, v_xpb_all)
    for wi, (w0, w1) in enumerate(WIN3):
        wl = w1 - w0
        bk = banks[4 + (wi % 2)]
        pe_acquire(bk)
        for j in range(DT):
            mm(pb[bk.idx][0:16, 0:wl], BWTB[:, j * 16:(j + 1) * 16],
               XPBj(j, w0, w1), j == 0, j == DT - 1, inc=(j == DT - 1))
        bk.w_pe = t_pe.n
        W("scalar", t_pe, bk.w_pe)
        P("scalar", lambda e, b=bk.idx, wl=wl: e.activation(
            out=SCR[0:16, 0:wl], in_=pb[b][0:16, 0:wl], func=AF.Exp))
        CB("scalar")
        P("scalar", lambda e, wl=wl: e.activation(
            out=SCR[0:16, 544:544 + wl], in_=SCR[0:16, 0:wl], func=AF.Ln,
            bias=1.0))
        CB("scalar")
        P("scalar", lambda e, w0=w0, w1=w1, wl=wl: e.copy(
            out=DB[:, w0:w1], in_=SCR[0:16, 544:544 + wl]))
        vv = M("scalar")
        bk.r_trk = (t_scl, vv)
    W("vector", t_scl, t_scl.n)
    P("vector", lambda e: e.tensor_mul(out=DB, in0=DB, in1=MASK))
    v_dbm = M("vector")
    W("tensor", t_vec, v_dbm)
    for wi, (w0, w1) in enumerate(WIN3):
        wl = w1 - w0
        bk = banks[4 + (wi % 2)]
        pe_acquire(bk)
        mm(pb[bk.idx][:, 0:wl], REP16, DB[:, w0:w1], True, True, inc=True)
        bk.w_pe = t_pe.n
        W("scalar", t_pe, bk.w_pe)
        P("scalar", lambda e, b=bk.idx, w0=w0, w1=w1, wl=wl: e.copy(
            out=DR[:, w0:w1], in_=pb[b][:, 0:wl]))
        vv = M("scalar")
        bk.r_trk = (t_scl, vv)
    v_dr = t_scl.n

    BAR()
    W("tensor", t_scl, v_dr)
    v_stageB = 0
    for k in range(2):
        gbase = 512 * k
        pieces = [(gbase, gbase + 512), (gbase + 512, min(gbase + 576, T))]
        if k == 1:
            W("tensor", t_vec, v_stageB)
        for g in range(NG):
            for pi, (p0, p1) in enumerate(pieces):
                pl = p1 - p0
                bk = banks[(g * 2 + pi) % 4]
                pe_acquire(bk)
                for j in range(DT):
                    mm(pb[bk.idx][:, 0:pl], PACK(j, g), XPBj(j, p0, p1),
                       j == 0, j == DT - 1, inc=(j == DT - 1))
                bk.w_pe = t_pe.n
                W("vector", t_pe, bk.w_pe)
                P("vector", lambda e, g=g, p0=p0, p1=p1, pl=pl, b=bk.idx,
                  gb=gbase: e.tensor_tensor(
                      out=GH(g, p0 - gb, p1 - gb), in0=pb[b][:, 0:pl],
                      in1=DR[:, p0:p1], op=ALU.mult))
                vv = M("vector")
                bk.r_trk = (t_vec, vv)
        CB("vector")
        P("vector", lambda e: e.tensor_add(
            out=ACC1[:, 0:519], in0=GH(0, 57, 576), in1=GH(1, 49, 568)))
        CB("vector")
        for g in range(2, NG):
            c0 = 57 - 8 * g
            P("vector", lambda e, c0=c0, g=g: e.tensor_add(
                out=ACC1[:, 0:519], in0=ACC1[:, 0:519], in1=GH(g, c0, c0 + 519)))
            CB("vector")
        v_stageA = M("vector")
        bkS = banks[4 + k % 2]
        W("tensor", t_vec, v_stageA)
        pe_acquire(bkS)
        for o in range(NO):
            mm(pb[bkS.idx][0:16, 0:512], IDENTF[:, 16 * o:16 * o + 16],
               ACC1[:, 7 - o:519 - o], o == 0, o == NO - 1, inc=(o == NO - 1))
        bkS.w_pe = t_pe.n
        W("vector", t_pe, bkS.w_pe)
        P("vector", lambda e, k=k, b=bkS.idx: e.tensor_copy(
            out=SH[:, 512 * k:512 * (k + 1)], in_=pb[b][0:16, 0:512]))
        v_stageB = M("vector")
        bkS.r_trk = (t_vec, v_stageB)
    CB("vector")
    P("vector", lambda e: e.tensor_copy(out=SHB, in_=SH))
    v_shb = M("vector")

    BAR()
    W("gpsimd", t_pe, t_pe.n)
    v_cwt = DMA("gpsimd", s_cwt, lambda e: e.dma_start(out=CWTB, in_=d_cwt))
    W("tensor", v_cwt)
    W("tensor", t_vec, v_shb)
    v_dscr_read = 0
    for t in range(DT):
        for wi in range(2):
            l0 = wi * 512
            bk = banks[(t * 2 + wi) % 4]
            pe_acquire(bk)
            mm(pb[bk.idx][:, 0:512], CWTB[:, t * 128:(t + 1) * 128],
               SHB[:, l0:l0 + 512], True, True, inc=True)
            bk.w_pe = t_pe.n
            W("scalar", t_vec, v_dscr_read)
            P("scalar", lambda e, t=t, l0=l0: e.activation(
                out=SCR[:, 0:512], in_=XPBj(t, PAD + l0, PAD + l0 + 512),
                func=AF.Copy, scale=DCOL[:, t:t + 1]))
            vsc = M("scalar")
            W("vector", t_pe, bk.w_pe)
            W("vector", t_scl, vsc)
            P("vector", lambda e, t=t, l0=l0, b=bk.idx: e.tensor_add(
                out=YPRE(t, l0, l0 + 512), in0=pb[b][:, 0:512],
                in1=SCR[:, 0:512]))
            vv = M("vector")
            bk.r_trk = (t_vec, vv)
            v_dscr_read = vv
    v_ypre = t_vec.n

    BAR()
    v_wout = {}
    W("gpsimd", t_pe, t_pe.n)
    for t in range(2):
        v_wout[t] = DMA("gpsimd", s_wro[t % 2],
                        lambda e, t=t: e.dma_start(out=WRO(t % 2),
                                                   in_=d_wout[t]))
    W("tensor", t_vec, v_ypre)
    v_yatt_prev = {0: 0, 1: 0}
    for t in range(DT):
        W("tensor", v_wout[t])
        for wi in range(2):
            l0 = wi * 512
            bk = banks[(t * 2 + wi) % 4]
            pe_acquire(bk)
            for j in range(DT):
                mm(pb[bk.idx][:, 0:512], WRO(t % 2)[:, j * 128:(j + 1) * 128],
                   YPRE(j, l0, l0 + 512), j == 0, j == DT - 1, inc=(j == DT - 1))
            bk.w_pe = t_pe.n
            sl = (t * 2 + wi) % 2
            W("scalar", t_pe, bk.w_pe)
            if t * 2 + wi >= 2:
                W("scalar", t_vec, v_yatt_prev[sl])
            P("scalar", lambda e, t=t, b=bk.idx, sl=sl: e.activation(
                out=SQT(sl)[:, 0:512], in_=pb[b][:, 0:512], func=AF.Identity,
                bias=BOUTC[:, t:t + 1]))
            vsc = M("scalar")
            bk.r_trk = (t_scl, vsc)
            W("vector", t_scl, vsc)
            P("vector", lambda e, t=t, l0=l0, sl=sl: e.tensor_add(
                out=YATT(t, l0, l0 + 512), in0=SQT(sl)[:, 0:512],
                in1=XN1j(t, PAD + l0, PAD + l0 + 512)))
            vv = M("vector")
            v_yatt_prev[sl] = vv
        if t + 2 < DT:
            W("gpsimd", t_pe, t_pe.n)
            v_wout[t + 2] = DMA("gpsimd", s_wro[t % 2],
                                lambda e, t=t: e.dma_start(out=WRO(t % 2),
                                                           in_=d_wout[t + 2]))
    v_yatt_all = t_vec.n

    BAR()
    WL2 = [(0, 512), (512, 1024)]

    def outer1_apply(wi, c0, c1, v_chain):
        wl = c1 - c0
        W("vector", t_scl, v_chain)
        for j in range(DT):
            P("vector", lambda e, j=j, c0=c0, wl=wl: e.tensor_sub(
                out=SQT(2)[:, 0:wl], in0=YATT(j, c0, c0 + wl),
                in1=MB_B[:, 0:wl]))
            CB("vector")
            P("vector", lambda e, j=j, c0=c0, wl=wl: e.tensor_mul(
                out=CH2[:, 0:wl], in0=SQT(2)[:, 0:wl], in1=INVB[:, 0:wl]))
            CB("vector")
            P("vector", lambda e, j=j, c0=c0, wl=wl: e.tensor_add(
                out=Xj(j, PAD + c0, PAD + c0 + wl),
                in0=Xj(j, PAD + c0, PAD + c0 + wl), in1=CH2[:, 0:wl]))
            CB("vector")
        ln_state["mb_busy"] = (t_vec, M("vector"))

    emit_ln(WL2, lambda j, c0, c1: YATT(j, c0, c1), False,
            [(None, MB_B, INVB)] * 2, outer1_apply, (t_vec, v_yatt_all))
    v_x2 = ln_state["mb_busy"][1]

    BAR()
    emit_ln(WL2, lambda j, c0, c1: Xj(j, PAD + c0, PAD + c1), True,
            [(MB_F, None, INVB), (MB_F2, None, INVB2)], None, (t_vec, v_x2))
    v_ln2bc = t_scl.n

    act_table("silu_and_others")

    BAR()
    v_wg = {}
    v_wu = {}
    v_wd = {}
    for k in range(2):
        l0 = 512 * k
        mbf = MB_F if k == 0 else MB_F2
        invb = INVB if k == 0 else INVB2
        W("vector", t_scl, v_ln2bc)
        for j in range(DT):
            P("vector", lambda e, j=j, l0=l0, mbf=mbf: e.tensor_sub(
                out=SQT(3)[:, 0:512], in0=Xj(j, PAD + l0, PAD + l0 + 512),
                in1=mbf[:, 0:512]))
            CB("vector")
            P("vector", lambda e, j=j, k=k, invb=invb: e.tensor_mul(
                out=X2NW(k, j), in0=SQT(3)[:, 0:512], in1=invb[:, 0:512]))
            CB("vector")
        v_x2n = M("vector")
        W("gpsimd", t_pe, t_pe.n)
        for i in range(2):
            v_wg[i] = DMA("gpsimd", s_wrf[i % 2],
                          lambda e, i=i: e.dma_start(out=WRS(i % 2),
                                                     in_=d_wg[i]))
            v_wu[i] = DMA("gpsimd", s_wrf[2 + i % 2],
                          lambda e, i=i: e.dma_start(out=WRS(2 + i % 2),
                                                     in_=d_wu[i]))
        W("tensor", t_vec, v_x2n)
        v_fuse_last = 0
        for i in range(IT):
            W("tensor", v_wg[i])
            W("tensor", v_wu[i])
            bg = banks[i % 2]
            bu = banks[2 + i % 2]
            pe_acquire(bg)
            pe_acquire(bu)
            for j in range(DT):
                mm(pb[bg.idx][:, 0:512], WRS(i % 2)[:, j * 128:(j + 1) * 128],
                   X2NW(k, j), j == 0, j == DT - 1, inc=(j == DT - 1))
            bg.w_pe = t_pe.n
            for j in range(DT):
                mm(pb[bu.idx][:, 0:512], WRS(2 + i % 2)[:, j * 128:(j + 1) * 128],
                   X2NW(k, j), j == 0, j == DT - 1, inc=(j == DT - 1))
            bu.w_pe = t_pe.n
            W("scalar", t_pe, bg.w_pe)
            if i >= 1:
                W("scalar", t_vec, v_fuse_last)
            P("scalar", lambda e, i=i, b=bg.idx: e.activation(
                out=GS, in_=pb[b][:, 0:512], func=AF.Silu,
                scale=SGC[:, i:i + 1]))
            vsc = M("scalar")
            bg.r_trk = (t_scl, vsc)
            W("vector", t_scl, vsc)
            W("vector", t_pe, bu.w_pe)
            P("vector", lambda e, i=i, b=bu.idx: e.scalar_tensor_tensor(
                out=FUSE(i), in0=pb[b][:, 0:512], scalar=SUC[:, i:i + 1],
                in1=GS, op0=ALU.mult, op1=ALU.mult))
            v_fuse_last = M("vector")
            bu.r_trk = (t_vec, v_fuse_last)
            if i + 2 < IT:
                W("gpsimd", t_pe, t_pe.n)
                v_wg[i + 2] = DMA("gpsimd", s_wrf[i % 2],
                                  lambda e, i=i: e.dma_start(
                                      out=WRS(i % 2), in_=d_wg[i + 2]))
                v_wu[i + 2] = DMA("gpsimd", s_wrf[2 + i % 2],
                                  lambda e, i=i: e.dma_start(
                                      out=WRS(2 + i % 2), in_=d_wu[i + 2]))
        W("gpsimd", t_pe, t_pe.n)
        for (tt, hh) in [(0, 0), (0, 1)]:
            v_wd[(tt, hh)] = DMA("gpsimd", s_wrd[(tt * 2 + hh) % 2],
                                 lambda e, t=tt, hf=hh: e.dma_start(
                                     out=WRD((t * 2 + hf) % 2),
                                     in_=d_wd[t, hf]))
        W("tensor", t_vec, v_fuse_last)
        for t in range(DT):
            bk = banks[4 + t % 2]
            pe_acquire(bk)
            for hf in range(2):
                W("tensor", v_wd[(t, hf)])
                for kk in range(32):
                    ki = hf * 32 + kk
                    mm(pb[bk.idx][:, 0:512],
                       WRD((t * 2 + hf) % 2)[:, kk * 128:(kk + 1) * 128],
                       FUSE(ki), ki == 0, ki == IT - 1, inc=(ki == IT - 1))
                nt, nh = (t, hf + 1) if hf == 0 else (t + 1, 0)
                nt2, nh2 = (nt, nh + 1) if nh == 0 else (nt + 1, 0)
                pre = (nt2, nh2)
                if pre[0] < DT and pre not in v_wd:
                    W("gpsimd", t_pe, t_pe.n)
                    v_wd[pre] = DMA("gpsimd", s_wrd[(pre[0] * 2 + pre[1]) % 2],
                                    lambda e, p=pre: e.dma_start(
                                        out=WRD((p[0] * 2 + p[1]) % 2),
                                        in_=d_wd[p[0], p[1]]))
            bk.w_pe = t_pe.n
            W("scalar", t_pe, bk.w_pe)
            P("scalar", lambda e, t=t, k=k, b=bk.idx: e.activation(
                out=Y2W(k, t), in_=pb[b][:, 0:512], func=AF.Identity,
                scale=SDC[:, t:t + 1]))
            vsc = M("scalar")
            bk.r_trk = (t_scl, vsc)
        v_wd = {}
    v_y2_all = t_scl.n

    act_table("natural_log_exp_and_others")

    BAR()
    d_out3 = d_out.rearrange("p (j c) -> p j c", j=DT)
    v_out_parts = []

    def final_apply(wi, c0, c1, v_chain):
        wl = c1 - c0
        W("vector", t_scl, v_chain)
        for j in range(DT):
            P("vector", lambda e, j=j, wi=wi, wl=wl: e.tensor_sub(
                out=SQT(2)[:, 0:wl], in0=Y2W(wi, j), in1=MB_B[:, 0:wl]))
            CB("vector")
            P("vector", lambda e, j=j, wl=wl: e.tensor_mul(
                out=CH2[:, 0:wl], in0=SQT(2)[:, 0:wl], in1=INVB[:, 0:wl]))
            CB("vector")
            P("vector", lambda e, j=j, c0=c0, wl=wl: e.tensor_add(
                out=Xj(j, PAD + c0, PAD + c0 + wl),
                in0=Xj(j, PAD + c0, PAD + c0 + wl), in1=CH2[:, 0:wl]))
            CB("vector")
        v_apply = M("vector")
        ln_state["mb_busy"] = (t_vec, v_apply)
        W("sync", t_vec, v_apply)
        v_out_parts.append(DMA("sync", s_out, lambda e, c0=c0, c1=c1:
                               e.dma_start(out=d_out3[:, :, c0:c1],
                                           in_=x_ar3[:, :, PAD + c0:PAD + c1])))

    emit_ln(WL2, lambda j, c0, c1: Y2W(0 if c0 == 0 else 1, j), False,
            [(None, MB_B, INVB)] * 2, final_apply, (t_scl, v_y2_all))

    def tap(name, view, wait_trk, wait_val):
        if name in tap_out:
            W("sync", wait_trk, wait_val)
            DMA("sync", s_tap, lambda e, d=tap_out[name], v=view:
                e.dma_start(out=d, in_=v))

    tap("xn1b", v_bf(o_xn1, 128, DT * T), t_vec, v_xn1_done)
    tap("x_projb", v_bf(o_xpb, 128, DT * T), t_scl, v_xpb_all)
    tap("deltab", DB, t_vec, v_dbm)
    tap("S_h", SH, t_vec, v_shb)
    tap("ypre0", v_bf(o_ypre[0], 128, TOUT), t_vec, v_ypre)
    tap("yatt", v_bf(o_xpb, 128, DT * T), t_vec, v_yatt_all)
    tap("x2", v_f32(o_x, 128, DT * T), t_vec, v_x2)
    tap("y2b", v_bf(o_y2a, 128, DT * 512), t_scl, v_y2_all)

    W("gpsimd", s_out, s_out.n)
    if s_tap.n:
        W("gpsimd", s_tap, s_tap.n)

    with nc.Block() as block:
        @block.sync
        def _(e):
            for f in plan["sync"]:
                f(e)

        @block.gpsimd
        def _(e):
            for f in plan["gpsimd"]:
                f(e)

        @block.scalar
        def _(e):
            for f in plan["scalar"]:
                f(e)

        @block.vector
        def _(e):
            for f in plan["vector"]:
                f(e)

        @block.tensor
        def _(e):
            for f in plan["tensor"]:
                f(e)

    return nc



def _prep_shared(inp):
    f32 = np.float32
    bf = ml_dtypes.bfloat16

    def tile_kblocks(w, nblk):
        K = w.shape[0]
        out = np.empty((nblk, 128, K), w.dtype)
        for t in range(nblk):
            blk = w[:, t * 128:(t + 1) * 128]
            out[t] = blk.reshape(K // 128, 128, 128).transpose(1, 0, 2).reshape(
                128, K)
        return np.ascontiguousarray(out)

    w_in = np.asarray(inp["L0_attn_proj_in_w"], f32)
    w_out = np.asarray(inp["L0_attn_proj_out_w"], f32)
    sh = {}
    sh["w_in_t"] = tile_kblocks(np.ascontiguousarray(w_in.T), DT)
    sh["w_out_t"] = tile_kblocks(np.ascontiguousarray(w_out.T), DT)
    bwt = np.ascontiguousarray(np.asarray(inp["L0_attn_B_w"], f32).T)
    sh["b_w_t"] = np.ascontiguousarray(
        bwt.reshape(16, 128, 16).transpose(1, 0, 2).reshape(128, 256))
    sh["c_w_t"] = np.ascontiguousarray(np.asarray(inp["L0_attn_C_w"], f32).T)
    a = np.asarray(inp["L0_attn_A"], f32)
    sh["a_t"] = np.ascontiguousarray(
        a.reshape(16, 128, 16).transpose(1, 0, 2).reshape(128, 256))
    for nm, key in [("dcol", "L0_attn_D_param"), ("bincol", "L0_attn_proj_in_b"),
                    ("boutcol", "L0_attn_proj_out_b"), ("sdcol", "L0_ffn_s_down")]:
        sh[nm] = np.ascontiguousarray(np.asarray(inp[key], f32).reshape(16, 128).T)
    for nm, key in [("sgcol", "L0_ffn_s_gate"), ("sucol", "L0_ffn_s_up")]:
        sh[nm] = np.ascontiguousarray(np.asarray(inp[key], f32).reshape(64, 128).T)
    sh["wg_t"] = tile_kblocks(np.asarray(inp["L0_ffn_w_gate"]), IT)
    sh["wu_t"] = tile_kblocks(np.asarray(inp["L0_ffn_w_up"]), IT)
    wd = np.asarray(inp["L0_ffn_w_down"])
    wd_t = np.empty((DT, 2, 128, 4096), np.int8)
    for t in range(DT):
        for hf in range(2):
            blk = wd[hf * 4096:(hf + 1) * 4096, t * 128:(t + 1) * 128]
            wd_t[t, hf] = blk.reshape(32, 128, 128).transpose(1, 0, 2).reshape(
                128, 4096)
    sh["wd_t"] = np.ascontiguousarray(wd_t)
    rep = np.zeros((16, 128), bf)
    for c in range(128):
        rep[c % 16, c] = 1.0
    sh["rep16b"] = rep
    sh["ones_b"] = np.ones((128, 1), bf)
    sh["ones_f"] = np.ones((1, 128), f32)
    sh["ood_f"] = np.full((1, 128), 1.0 / D, f32)
    sh["ident_f"] = np.eye(128, dtype=f32)
    return sh


def _prep_core(x, b, h):
    f32 = np.float32
    bf = ml_dtypes.bfloat16
    xt = np.ascontiguousarray(x[b].T.astype(f32))
    if h == 0:
        sl = np.concatenate([np.zeros((D, PAD), f32), xt[:, 0:TOUT]], axis=1)
    else:
        sl = xt[:, TOUT - PAD:2 * TOUT]
    xT = np.ascontiguousarray(
        sl.reshape(16, 128, T).transpose(1, 0, 2).reshape(128, DT * T))
    mask = np.ones((16, T), bf)
    if h == 0:
        mask[:, 0:PAD] = 0
    return {"xT": xT, "maskb": mask}


def kernel(**inputs):
    x = np.asarray(inputs["L0_attn_x"], np.float32)
    sh = _prep_shared(inputs)
    in_maps = []
    for c in range(8):
        b, h = divmod(c, 2)
        m = dict(sh)
        m.update(_prep_core(x, b, h))
        in_maps.append(m)
    nc = build_program()
    res = run_bass_kernel_spmd(nc, in_maps, core_ids=list(range(8)))
    out = np.empty((4, 2048, 2048), np.float32)
    for c in range(8):
        b, h = divmod(c, 2)
        o = np.asarray(res.results[c]["out_xT"])
        ot = o.reshape(128, DT, TOUT).transpose(1, 0, 2).reshape(D, TOUT)
        out[b, h * TOUT:(h + 1) * TOUT, :] = ot.T
    return out


# version 02
# speedup vs baseline: 1.0015x; 1.0015x over previous
import contextlib
import numpy as np
import ml_dtypes
import concourse.bass as bass
import concourse.mybir as mybir
from concourse.bass_utils import run_bass_kernel_spmd

F32 = mybir.dt.float32
BF16 = mybir.dt.bfloat16
I8 = mybir.dt.int8
U8 = mybir.dt.uint8
AF = mybir.ActivationFunctionType
ALU = mybir.AluOpType

D = 2048
DT = 16
S = 16
TOUT = 1024
PAD = 64
T = PAD + TOUT
NO = 8
NG = 6
INTER = 8192
IT = 64
WIN3 = [(0, 512), (512, 1024), (1024, T)]
EPS = 1e-5
SBUF_BYTES = 212832

ENGINES = ("sync", "gpsimd", "scalar", "vector", "tensor")


class Trk:
    def __init__(self, name, step=1):
        self.name, self.step, self.n, self.h = name, step, 0, None


class Alloc:
    def __init__(self, nbytes):
        self.cap = nbytes
        self.items = []

    def alloc(self, size, t0, t1, name):
        size = (size + 63) & ~63
        spans = sorted((o, o + s) for (o, s, a0, a1, _n) in self.items
                       if not (a1 <= t0 or t1 <= a0))
        off = 0
        for (s0, s1) in spans:
            if off + size <= s0:
                break
            off = max(off, s1)
        if off + size > self.cap:
            raise MemoryError(f"SBUF overflow: {name} {size}B t[{t0},{t1}] off {off}")
        self.items.append((off, size, t0, t1, name))
        return off


def build_program(taps=()):
    nc = bass.Bass("TRN2", target_bir_lowering=False, debug=False)

    d_x = nc.dram_tensor("xT", [128, DT * T], F32, kind="ExternalInput").ap()
    d_win = nc.dram_tensor("w_in_t", [DT, 128, 2048], F32, kind="ExternalInput").ap()
    d_wout = nc.dram_tensor("w_out_t", [DT, 128, 2048], F32, kind="ExternalInput").ap()
    d_bwt = nc.dram_tensor("b_w_t", [128, 256], F32, kind="ExternalInput").ap()
    d_cwt = nc.dram_tensor("c_w_t", [16, 2048], F32, kind="ExternalInput").ap()
    d_a = nc.dram_tensor("a_t", [128, 256], F32, kind="ExternalInput").ap()
    d_dcol = nc.dram_tensor("dcol", [128, 16], F32, kind="ExternalInput").ap()
    d_bin = nc.dram_tensor("bincol", [128, 16], F32, kind="ExternalInput").ap()
    d_bout = nc.dram_tensor("boutcol", [128, 16], F32, kind="ExternalInput").ap()
    d_sd = nc.dram_tensor("sdcol", [128, 16], F32, kind="ExternalInput").ap()
    d_sg = nc.dram_tensor("sgcol", [128, 64], F32, kind="ExternalInput").ap()
    d_su = nc.dram_tensor("sucol", [128, 64], F32, kind="ExternalInput").ap()
    d_wg = nc.dram_tensor("wg_t", [IT, 128, 2048], I8, kind="ExternalInput").ap()
    d_wu = nc.dram_tensor("wu_t", [IT, 128, 2048], I8, kind="ExternalInput").ap()
    d_wd = nc.dram_tensor("wd_t", [DT, 2, 128, 4096], I8, kind="ExternalInput").ap()
    d_mask = nc.dram_tensor("maskb", [16, T], BF16, kind="ExternalInput").ap()
    d_rep16 = nc.dram_tensor("rep16b", [16, 128], BF16, kind="ExternalInput").ap()
    d_ones_b = nc.dram_tensor("ones_b", [128, 1], BF16, kind="ExternalInput").ap()
    d_ones_f = nc.dram_tensor("ones_f", [1, 128], F32, kind="ExternalInput").ap()
    d_oodf = nc.dram_tensor("ood_f", [1, 128], F32, kind="ExternalInput").ap()
    d_identf = nc.dram_tensor("ident_f", [128, 128], F32, kind="ExternalInput").ap()
    d_out = nc.dram_tensor("out_xT", [128, DT * TOUT], F32, kind="ExternalOutput").ap()
    tap_out = {}
    for tname, tshape, tdt in taps:
        tap_out[tname] = nc.dram_tensor("tap_" + tname, tshape, tdt,
                                        kind="ExternalOutput").ap()

    al = Alloc(SBUF_BYTES)
    o_x = al.alloc(DT * T * 4, 0, 11, "x")
    o_srow = al.alloc(2 * T * 4, 0, 11, "sumrows")
    o_identf = al.alloc(128 * 4, 0, 11, "identf")
    o_scr = al.alloc(1088 * 4, 0, 11, "scratch")
    o_sq = al.alloc(4 * 512 * 2, 0, 11, "sqscr")
    o_const = al.alloc(4 * 64 + 2 * 256 + 256 + 64, 0, 11, "consts")
    o_onesf = al.alloc(2 * 128 * 4, 0, 11, "onesf+oodf")
    o_dly = al.alloc(512 * 2, 0, 11, "delay")
    o_bcb = al.alloc(2 * 2048 + 3 * 1024, 5, 11, "bcastB")
    o_y2a = al.alloc(DT * 512 * 2, 7, 11, "y2b_w0")
    o_y2b = al.alloc(DT * 512 * 2, 9, 11, "y2b_w1")
    o_xn1 = al.alloc(DT * T * 2, 0, 6, "xn1b")
    o_xpb = al.alloc(DT * T * 2, 1, 6, "x_projb/y_attnb")
    o_pk = al.alloc(DT * NG * 128 * 2, 0, 4, "packs")
    o_gh = al.alloc(NG * 576 * 4, 3, 4, "ghat")
    o_dr = al.alloc(T * 4, 2, 4, "deltar")
    o_db = al.alloc(T * 2, 2, 3, "deltab")
    o_mask = al.alloc(T * 2, 0, 3, "maskb")
    o_acc1 = al.alloc(520 * 4, 3, 4, "acc1")
    o_sh = al.alloc(TOUT * 4, 3, 4, "S_h")
    o_shb = al.alloc(TOUT * 2, 3, 5, "S_hb")
    o_bca = al.alloc(2048 + 1024, 0, 2, "bcastA")
    o_wri = al.alloc(2 * 4096, 0, 2, "wring_in")
    o_wro = al.alloc(2 * 4096, 5, 6, "wring_out")
    o_wrf = al.alloc(4 * 4096, 6, 10, "wring_ffn")
    o_bwt = al.alloc(256 * 2, 0, 3, "bwtb")
    o_cwt = al.alloc(2048 * 2, 4, 5, "cwtb")
    o_sig = al.alloc(2 * 1024, 0, 1, "sig")
    o_pkf = al.alloc(2 * 8192, 0, 1, "pkf")
    o_ypre = [al.alloc(TOUT * 2, 4, 6, f"ypre{t}") for t in range(DT)]
    o_x2na = al.alloc(DT * 512 * 2, 6, 7, "x2nb_w0")
    o_x2nb = al.alloc(DT * 512 * 2, 8, 9, "x2nb_w1")
    o_fuse = [al.alloc(512 * 2, 6, 10, f"fuse{k}") for k in range(IT)]
    import os as _os
    if _os.environ.get("KDEBUG_ALLOC"):
        for (o, s, a0, a1, n) in sorted(al.items):
            print(f"{o:7d} +{s:6d} t[{a0:2d},{a1:2d}] {n}")
        raise SystemExit(0)

    plan = {e: [] for e in ENGINES}

    def Praw(eng, fn):
        plan[eng].append(fn)

    t_pe, t_vec, t_scl = Trk("t_pe"), Trk("t_vec"), Trk("t_scl")
    s_const = Trk("s_const", 16)
    s_x = [Trk(f"s_x{i}", 16) for i in range(3)]
    s_bwt, s_cwt = Trk("s_bwt", 16), Trk("s_cwt", 16)
    s_wri = [Trk(f"s_wri{i}", 16) for i in range(2)]
    s_wro = [Trk(f"s_wro{i}", 16) for i in range(2)]
    s_wrf = [Trk(f"s_wrf{i}", 16) for i in range(4)]
    s_wrd = [Trk(f"s_wrd{i}", 16) for i in range(2)]
    s_out, s_tap = Trk("s_out", 16), Trk("s_tap", 16)
    DMAT = ([s_const] + s_x + [s_bwt, s_cwt] + s_wri + s_wro + s_wrf + s_wrd
            + [s_out, s_tap])
    TRKS = [t_pe, t_vec, t_scl] + DMAT
    ETRK = {"tensor": t_pe, "vector": t_vec, "scalar": t_scl}

    def DMA(queue, trk, build):
        trk.n += trk.step
        plan[queue].append(lambda e, b=build, t=trk: b(e).then_inc(t.h, t.step))
        return (trk, trk.n)

    def W(eng, trk, val=None):
        if val is None:
            trk, val = trk
        if val and val > 0:
            Praw(eng, lambda e, t=trk, v=val: e.wait_ge(t.h, v))

    def P(eng, fn):
        trk = ETRK[eng]
        trk.n += trk.step
        plan[eng].append(lambda e, b=fn, t=trk: b(e).then_inc(t.h, t.step))
        return trk.n

    def E(eng, build, inc=False):
        if inc:
            return P(eng, build)
        Praw(eng, build)
        return None

    def M(eng):
        return ETRK[eng].n

    def CB(eng):
        trk = ETRK[eng]
        Praw(eng, lambda e, t=trk, v=trk.n: e.wait_ge(t.h, v))

    def BAR():
        for eng in ("tensor", "vector", "scalar"):
            for trk in (t_pe, t_vec, t_scl):
                if trk.n > 0:
                    Praw(eng, lambda e, t=trk, v=trk.n: e.wait_ge(t.h, v))

    class Bank:
        def __init__(self, idx):
            self.idx, self.w_pe, self.r_trk = idx, 0, None

    banks = [Bank(i) for i in range(8)]

    def pe_acquire(bk):
        if bk.r_trk is not None:
            W("tensor", bk.r_trk[0], bk.r_trk[1])

    def mm(out_ap, lhsT, rhs, start, stop, inc=True):
        b = (lambda e, o=out_ap, l=lhsT, r=rhs, st=start, sp=stop:
             e.matmul(o, l, r, start=st, stop=sp))
        t_pe.n += 1
        plan["tensor"].append(lambda e, bb=b: bb(e).then_inc(t_pe.h, 1))
        return t_pe.n

    def act_table(set_name):
        from concourse.hw_specs import get_activation_tables
        sid = list(get_activation_tables(nc.m.arch).keys()).index(set_name)

        Praw("scalar", lambda e, sid=sid: e.add_instruction(
            mybir.InstLoadActFuncSet(
                name=nc.get_next_instruction_name(), act_func_set_id=sid,
                ins=[], outs=[])))
        for _ in range(12):
            P("scalar", lambda e: e.copy(out=DLY, in_=DLY))
            CB("scalar")

    es = contextlib.ExitStack()
    nc._kernel_es = es
    ar = es.enter_context(nc.sbuf_tensor("arena", [128, SBUF_BYTES], U8))
    pb = [es.enter_context(nc.psum_tensor(f"pb{i}", [128, 512], F32))
          for i in range(8)]
    for trk in TRKS:
        trk.h = es.enter_context(nc.semaphore(trk.name))

    arena_base = None
    for a in nc.m.functions[0].allocations:
        mls = getattr(a, "memorylocations", None)
        if not mls:
            continue
        if mls[0].name == "arena":
            arena_base = mls[0].addr
    assert arena_base is not None

    def REG(name, off, part, cols, dtype):
        return nc.alloc_sbuf_tensor_at(name, [part, cols], dtype,
                                       offset=arena_base + off).ap()

    XT = REG("XT", o_x, 128, DT * T, F32)
    XN1T = REG("XN1T", o_xn1, 128, DT * T, BF16)
    XPBT = REG("XPBT", o_xpb, 128, DT * T, BF16)
    PKT = REG("PKT", o_pk, 128, DT * NG * 128, BF16)
    GHT = REG("GHT", o_gh, 128, NG * 576, F32)
    DR = REG("DRT", o_dr, 128, T, F32)
    DB = REG("DBT", o_db, 16, T, BF16)
    MASK = REG("MASKT", o_mask, 16, T, BF16)
    ACC1 = REG("ACC1T", o_acc1, 128, 520, F32)
    SH = REG("SHT", o_sh, 16, TOUT, F32)
    SHB = REG("SHBT", o_shb, 16, TOUT, BF16)
    SCRT = REG("SCRT", o_scr, 128, 1088, F32)
    SCR = SCRT
    SRT = REG("SRT", o_srow, 1, 2 * T, F32)
    SRA = SRT[:, 0:T]
    SRB = SRT[:, T:2 * T]
    CHS = SCRT[:, 0:512]
    CH2 = SCRT[:, 544:1056]
    GS = SCRT[:, 0:512]
    SQTT = REG("SQTT", o_sq, 128, 2048, BF16)

    def SQT(i):
        return SQTT[:, i * 512:(i + 1) * 512]

    MBFT = REG("MBFT", o_bcb, 128, 1024, F32)
    MB_F = MBFT[:, 0:512]
    MB_F2 = MBFT[:, 512:1024]
    INVBT = REG("INVBT", o_bcb + 4096, 128, 1536, BF16)
    INVB = INVBT[:, 0:512]
    INVB2 = INVBT[:, 512:1024]
    MB_B = INVBT[:, 1024:1536]
    MB1F = REG("MB1F", o_bca, 128, 512, F32)
    INV1B = REG("INV1B", o_bca + 2048, 128, 512, BF16)
    WRIT = REG("WRIT", o_wri, 128, 4096, BF16)
    WROT = REG("WROT", o_wro, 128, 4096, BF16)
    WRFT = REG("WRFT", o_wrf, 128, 8192, BF16)

    def WRI(i):
        return WRIT[:, i * 2048:(i + 1) * 2048]

    def WRO(i):
        return WROT[:, i * 2048:(i + 1) * 2048]

    def WRS(i):
        return WRFT[:, i * 2048:(i + 1) * 2048]

    def WRD(i):
        return WRFT[:, i * 4096:(i + 1) * 4096]

    BWTB = REG("BWTT", o_bwt, 128, 256, BF16)
    CWTB = REG("CWTT", o_cwt, 16, 2048, BF16)
    oc = o_const
    DCOL = REG("DCOL", oc, 128, 16, F32); oc += 64
    BINC = REG("BINC", oc, 128, 16, F32); oc += 64
    BOUTC = REG("BOUTC", oc, 128, 16, F32); oc += 64
    SDC = REG("SDC", oc, 128, 16, F32); oc += 64
    SGC = REG("SGC", oc, 128, 64, F32); oc += 256
    SUC = REG("SUC", oc, 128, 64, F32); oc += 256
    REP16 = REG("REP16", oc, 16, 128, BF16); oc += 256
    ONESB = REG("ONESB", oc, 128, 1, BF16); oc += 64
    ONESF = REG("ONESF", o_onesf, 1, 128, F32)
    OODF = REG("OODF", o_onesf + 512, 1, 128, F32)
    IDENTF = REG("IDENTF", o_identf, 128, 128, F32)
    DLY = REG("DLY", o_dly, 128, 512, BF16)
    SIGA = REG("SIGA", o_sig, 128, 256, F32)
    SIG8 = REG("SIG8", o_sig + 1024, 128, 256, F32)
    PKF = REG("PKF", o_pkf, 128, 2048, F32)
    S8R = REG("S8R", o_pkf + 8192, 128, 2048, F32)
    YPRET = [REG(f"YPRE{t}", o_ypre[t], 128, TOUT, BF16) for t in range(DT)]
    FUSET = [REG(f"FUSE{k}", o_fuse[k], 128, 512, BF16) for k in range(IT)]
    X2NAT = REG("X2NAT", o_x2na, 128, DT * 512, BF16)
    X2NBT = REG("X2NBT", o_x2nb, 128, DT * 512, BF16)
    Y2AT = REG("Y2AT", o_y2a, 128, DT * 512, BF16)
    Y2BT = REG("Y2BT", o_y2b, 128, DT * 512, BF16)

    def Xj(j, c0, c1):
        return XT[:, j * T + c0:j * T + c1]

    def XN1j(j, c0, c1):
        return XN1T[:, j * T + c0:j * T + c1]

    def XPBj(j, c0, c1):
        return XPBT[:, j * T + c0:j * T + c1]

    def YATT(j, l0, l1):
        return XPBT[:, j * T + l0:j * T + l1]

    def PACK(j, g):
        return PKT[:, (j * NG + g) * 128:(j * NG + g + 1) * 128]

    def GH(g, c0, c1):
        return GHT[:, g * 576 + c0:g * 576 + c1]

    def YPRE(t, l0, l1):
        return YPRET[t][:, l0:l1]

    def FUSE(k):
        return FUSET[k]

    def X2NW(k, j):
        t = X2NAT if k == 0 else X2NBT
        return t[:, j * 512:(j + 1) * 512]

    def Y2W(k, j):
        t = Y2AT if k == 0 else Y2BT
        return t[:, j * 512:(j + 1) * 512]

    for (dst, src) in [(SIGA, d_a), (DCOL, d_dcol), (BINC, d_bin),
                       (BOUTC, d_bout), (SDC, d_sd), (SGC, d_sg), (SUC, d_su),
                       (MASK, d_mask), (REP16, d_rep16), (ONESB, d_ones_b),
                       (ONESF, d_ones_f), (OODF, d_oodf), (IDENTF, d_identf)]:
        DMA("sync", s_const, lambda e, d=dst, s=src: e.dma_start(out=d, in_=s))
    v_consts = (s_const, s_const.n)
    d_x3 = d_x.rearrange("p (j c) -> p j c", j=DT)
    x_ar3 = XT.rearrange("p (j c) -> p j c", j=DT)
    v_xw = []
    for wi, (w0, w1) in enumerate(WIN3):
        v_xw.append(DMA("sync", s_x[wi],
                        lambda e, a=x_ar3[:, :, w0:w1], s=d_x3[:, :, w0:w1]:
                        e.dma_start(out=a, in_=s)))
    v_bwt = DMA("gpsimd", s_bwt, lambda e: e.dma_start(out=BWTB, in_=d_bwt))
    v_win = {}
    for t in range(2):
        v_win[t] = DMA("gpsimd", s_wri[t % 2],
                       lambda e, t=t: e.dma_start(out=WRI(t % 2), in_=d_win[t]))

    P("vector", lambda e: e.memset(DLY, 0.0))
    v_dly = M("vector")
    W("scalar", t_vec, v_dly)
    act_table("natural_log_exp_and_others")

    W("vector", v_consts)
    W("scalar", v_consts)
    P("scalar", lambda e: e.activation(out=SCR[:, 0:256], in_=SIGA, func=AF.Exp,
                                       scale=-1.0))
    v_expA = M("scalar")
    W("vector", t_scl, v_expA)
    P("vector", lambda e: e.tensor_scalar_add(out=SCR[:, 0:256],
                                              in0=SCR[:, 0:256], scalar1=1.0))
    CB("vector")
    P("vector", lambda e: e.reciprocal(out=SIGA, in_=SCR[:, 0:256]))
    CB("vector")
    P("vector", lambda e: e.tensor_mul(out=SCR[:, 0:256], in0=SIGA, in1=SIGA))
    CB("vector")
    P("vector", lambda e: e.tensor_mul(out=SCR[:, 256:512], in0=SCR[:, 0:256],
                                       in1=SCR[:, 0:256]))
    CB("vector")
    P("vector", lambda e: e.tensor_mul(out=SIG8, in0=SCR[:, 256:512],
                                       in1=SCR[:, 256:512]))
    CB("vector")
    s8r4 = S8R.rearrange("p (j o s) -> p j o s", j=16, o=8)
    sig8_3 = SIG8.rearrange("p (j s) -> p j s", j=16)
    for o in range(NO):
        P("vector", lambda e, o=o: e.tensor_copy(out=s8r4[:, :, o, :], in_=sig8_3))
    CB("vector")
    pkf4 = PKF.rearrange("p (j o s) -> p j o s", j=16, o=8)
    siga3 = SIGA.rearrange("p (j s) -> p j s", j=16)
    P("vector", lambda e: e.memset(pkf4[:, :, 0, :], 1.0))
    CB("vector")
    for o in range(1, NO):
        P("vector", lambda e, o=o: e.tensor_tensor(
            out=pkf4[:, :, o, :], in0=pkf4[:, :, o - 1, :], in1=siga3,
            op=ALU.mult))
        CB("vector")
    v_pkf0 = M("vector")
    pk5 = PKT.rearrange("p (j g c) -> p j g c", j=DT, g=NG)
    pkf3 = PKF.rearrange("p (j c) -> p j c", j=16)
    W("scalar", t_vec, v_pkf0)
    for g in range(NG):
        P("scalar", lambda e, g=g: e.copy(out=pk5[:, :, g, :], in_=pkf3))
        if g < NG - 1:
            vv = M("scalar")
            W("vector", t_scl, vv)
            P("vector", lambda e: e.tensor_mul(out=PKF, in0=PKF, in1=S8R))
            CB("vector")
            vv2 = M("vector")
            W("scalar", t_vec, vv2)
    M("scalar")

    ln_state = {"mb_busy": (t_vec, 0)}

    def emit_ln(windows, src_fn, src_is_f32, outs, apply_fn, wait_src):
        for wi, (c0, c1) in enumerate(windows):
            wl = c1 - c0
            pe_acquire(banks[6])
            pe_acquire(banks[7])
            pe_j = {-2: t_pe.n, -1: t_pe.n}
            if wait_src is not None:
                ws = wait_src if isinstance(wait_src, list) else [wait_src]
                for pair in ws:
                    W("scalar", pair)
            for j in range(DT):
                W("scalar", t_pe, pe_j[j - 2])
                if src_is_f32:
                    sl = (j % 2) * 2
                    P("scalar", lambda e, j=j, sl=sl, c0=c0, c1=c1:
                      e.copy(out=SQT(sl)[:, 0:c1 - c0], in_=src_fn(j, c0, c1)))
                    P("scalar", lambda e, j=j, sl=sl, c0=c0, c1=c1:
                      e.activation(out=SQT(sl + 1)[:, 0:c1 - c0],
                                   in_=src_fn(j, c0, c1), func=AF.Square))
                    rhs_sum, rhs_sq = SQT(sl)[:, 0:wl], SQT(sl + 1)[:, 0:wl]
                else:
                    sl = j % 2
                    P("scalar", lambda e, j=j, sl=sl, c0=c0, c1=c1:
                      e.activation(out=SQT(sl)[:, 0:c1 - c0],
                                   in_=src_fn(j, c0, c1), func=AF.Square))
                    rhs_sum, rhs_sq = src_fn(j, c0, c1), SQT(sl)[:, 0:wl]
                vs = M("scalar")
                W("tensor", t_scl, vs)
                mm(pb[6][0:1, 0:wl], ONESB, rhs_sum, j == 0, j == DT - 1)
                pe_j[j] = mm(pb[7][0:1, 0:wl], ONESB, rhs_sq,
                             j == 0, j == DT - 1, inc=True)
            banks[6].w_pe = banks[7].w_pe = t_pe.n
            W("vector", t_pe, t_pe.n)
            P("vector", lambda e, c0=c0, c1=c1, wl=wl: e.tensor_copy(
                out=SRA[0:1, c0:c1], in_=pb[6][0:1, 0:wl]))
            P("vector", lambda e, c0=c0, c1=c1, wl=wl: e.tensor_copy(
                out=SRB[0:1, c0:c1], in_=pb[7][0:1, 0:wl]))
            vv = M("vector")
            banks[6].r_trk = banks[7].r_trk = (t_vec, vv)
        v_rows = t_vec.n
        for wi, (c0, c1) in enumerate(windows):
            wl = c1 - c0
            mbf, mbb, invb = outs[wi]
            W("tensor", t_vec, v_rows)
            pe_acquire(banks[6])
            mm(pb[6][:, 0:wl], OODF, SRA[0:1, c0:c1], True, True, inc=True)
            banks[6].w_pe = t_pe.n
            W("scalar", t_pe, banks[6].w_pe)
            W("scalar", ln_state["mb_busy"][0], ln_state["mb_busy"][1])
            CB("scalar")
            P("scalar", lambda e, wl=wl: e.copy(out=CHS[:, 0:wl],
                                                in_=pb[6][:, 0:wl]))
            if mbf is not None:
                P("scalar", lambda e, d=mbf, wl=wl: e.copy(out=d[:, 0:wl],
                                                           in_=pb[6][:, 0:wl]))
            if mbb is not None:
                P("scalar", lambda e, d=mbb, wl=wl: e.copy(out=d[:, 0:wl],
                                                           in_=pb[6][:, 0:wl]))
            v_m = M("scalar")
            banks[6].r_trk = (t_scl, v_m)
            pe_acquire(banks[7])
            mm(pb[7][:, 0:wl], OODF, SRB[0:1, c0:c1], True, True, inc=True)
            banks[7].w_pe = t_pe.n
            W("vector", t_scl, v_m)
            P("vector", lambda e, wl=wl: e.tensor_mul(
                out=CH2[:, 0:wl], in0=CHS[:, 0:wl], in1=CHS[:, 0:wl]))
            CB("vector")
            W("vector", t_pe, banks[7].w_pe)
            P("vector", lambda e, wl=wl: e.tensor_sub(
                out=CHS[:, 0:wl], in0=pb[7][:, 0:wl], in1=CH2[:, 0:wl]))
            CB("vector")
            P("vector", lambda e, wl=wl: e.tensor_scalar_add(
                out=CHS[:, 0:wl], in0=CHS[:, 0:wl], scalar1=EPS))
            vv2 = M("vector")
            banks[7].r_trk = (t_vec, vv2)
            W("scalar", t_vec, vv2)
            P("scalar", lambda e, wl=wl: e.activation(
                out=CHS[:, 0:wl], in_=CHS[:, 0:wl], func=AF.Ln))
            CB("scalar")
            P("scalar", lambda e, d=invb, wl=wl: e.activation(
                out=d[:, 0:wl], in_=CHS[:, 0:wl], func=AF.Exp, scale=-0.5))
            v_chain = M("scalar")
            if apply_fn is not None:
                apply_fn(wi, c0, c1, v_chain)

    def ln1_apply(wi, c0, c1, v_chain):
        wl = c1 - c0
        W("vector", t_scl, v_chain)
        for j in range(DT):
            P("vector", lambda e, j=j, c0=c0, c1=c1, wl=wl: e.tensor_sub(
                out=SQT(3)[:, 0:wl], in0=Xj(j, c0, c1), in1=MB1F[:, 0:wl]))
            CB("vector")
            P("vector", lambda e, j=j, c0=c0, c1=c1, wl=wl: e.tensor_mul(
                out=XN1j(j, c0, c1), in0=SQT(3)[:, 0:wl], in1=INV1B[:, 0:wl]))
            CB("vector")
        ln_state["mb_busy"] = (t_vec, M("vector"))

    emit_ln(WIN3, lambda j, c0, c1: Xj(j, c0, c1), True,
            [(MB1F, None, INV1B)] * 3, ln1_apply, v_xw)
    v_xn1_done = ln_state["mb_busy"][1]

    BAR()
    W("tensor", t_vec, v_xn1_done)
    for t in range(DT):
        W("tensor", v_win[t])
        for wi, (w0, w1) in enumerate(WIN3):
            wl = w1 - w0
            bk = banks[(t * 3 + wi) % 4]
            pe_acquire(bk)
            for j in range(DT):
                mm(pb[bk.idx][:, 0:wl], WRI(t % 2)[:, j * 128:(j + 1) * 128],
                   XN1j(j, w0, w1), j == 0, j == DT - 1, inc=(j == DT - 1))
            bk.w_pe = t_pe.n
            W("scalar", t_pe, bk.w_pe)
            P("scalar", lambda e, t=t, w0=w0, w1=w1, wl=wl, b=bk.idx:
              e.activation(out=XPBj(t, w0, w1), in_=pb[b][:, 0:wl],
                           func=AF.Identity, bias=BINC[:, t:t + 1]))
            vv = M("scalar")
            bk.r_trk = (t_scl, vv)
        if t + 2 < DT:
            W("gpsimd", t_pe, t_pe.n)
            v_win[t + 2] = DMA("gpsimd", s_wri[t % 2],
                               lambda e, t=t: e.dma_start(out=WRI(t % 2),
                                                          in_=d_win[t + 2]))
    v_xpb_all = t_scl.n

    BAR()
    W("tensor", v_bwt)
    W("tensor", t_scl, v_xpb_all)
    for wi, (w0, w1) in enumerate(WIN3):
        wl = w1 - w0
        bk = banks[4 + (wi % 2)]
        pe_acquire(bk)
        for j in range(DT):
            mm(pb[bk.idx][0:16, 0:wl], BWTB[:, j * 16:(j + 1) * 16],
               XPBj(j, w0, w1), j == 0, j == DT - 1, inc=(j == DT - 1))
        bk.w_pe = t_pe.n
        W("scalar", t_pe, bk.w_pe)
        P("scalar", lambda e, b=bk.idx, wl=wl: e.activation(
            out=SCR[0:16, 0:wl], in_=pb[b][0:16, 0:wl], func=AF.Exp))
        CB("scalar")
        P("scalar", lambda e, wl=wl: e.activation(
            out=SCR[0:16, 544:544 + wl], in_=SCR[0:16, 0:wl], func=AF.Ln,
            bias=1.0))
        CB("scalar")
        P("scalar", lambda e, w0=w0, w1=w1, wl=wl: e.copy(
            out=DB[:, w0:w1], in_=SCR[0:16, 544:544 + wl]))
        vv = M("scalar")
        bk.r_trk = (t_scl, vv)
    W("vector", t_scl, t_scl.n)
    P("vector", lambda e: e.tensor_mul(out=DB, in0=DB, in1=MASK))
    v_dbm = M("vector")
    W("tensor", t_vec, v_dbm)
    for wi, (w0, w1) in enumerate(WIN3):
        wl = w1 - w0
        bk = banks[4 + (wi % 2)]
        pe_acquire(bk)
        mm(pb[bk.idx][:, 0:wl], REP16, DB[:, w0:w1], True, True, inc=True)
        bk.w_pe = t_pe.n
        W("scalar", t_pe, bk.w_pe)
        P("scalar", lambda e, b=bk.idx, w0=w0, w1=w1, wl=wl: e.copy(
            out=DR[:, w0:w1], in_=pb[b][:, 0:wl]))
        vv = M("scalar")
        bk.r_trk = (t_scl, vv)
    v_dr = t_scl.n

    BAR()
    W("tensor", t_scl, v_dr)
    v_stageB = 0
    for k in range(2):
        gbase = 512 * k
        pieces = [(gbase, gbase + 512), (gbase + 512, min(gbase + 576, T))]
        if k == 1:
            W("tensor", t_vec, v_stageB)
        for g in range(NG):
            for pi, (p0, p1) in enumerate(pieces):
                pl = p1 - p0
                bk = banks[(g * 2 + pi) % 4]
                pe_acquire(bk)
                for j in range(DT):
                    mm(pb[bk.idx][:, 0:pl], PACK(j, g), XPBj(j, p0, p1),
                       j == 0, j == DT - 1, inc=(j == DT - 1))
                bk.w_pe = t_pe.n
                W("vector", t_pe, bk.w_pe)
                P("vector", lambda e, g=g, p0=p0, p1=p1, pl=pl, b=bk.idx,
                  gb=gbase: e.tensor_tensor(
                      out=GH(g, p0 - gb, p1 - gb), in0=pb[b][:, 0:pl],
                      in1=DR[:, p0:p1], op=ALU.mult))
                vv = M("vector")
                bk.r_trk = (t_vec, vv)
        CB("vector")
        P("vector", lambda e: e.tensor_add(
            out=ACC1[:, 0:519], in0=GH(0, 57, 576), in1=GH(1, 49, 568)))
        CB("vector")
        for g in range(2, NG):
            c0 = 57 - 8 * g
            P("vector", lambda e, c0=c0, g=g: e.tensor_add(
                out=ACC1[:, 0:519], in0=ACC1[:, 0:519], in1=GH(g, c0, c0 + 519)))
            CB("vector")
        v_stageA = M("vector")
        bkS = banks[4 + k % 2]
        W("tensor", t_vec, v_stageA)
        pe_acquire(bkS)
        for o in range(NO):
            mm(pb[bkS.idx][0:16, 0:512], IDENTF[:, 16 * o:16 * o + 16],
               ACC1[:, 7 - o:519 - o], o == 0, o == NO - 1, inc=(o == NO - 1))
        bkS.w_pe = t_pe.n
        W("vector", t_pe, bkS.w_pe)
        P("vector", lambda e, k=k, b=bkS.idx: e.tensor_copy(
            out=SH[:, 512 * k:512 * (k + 1)], in_=pb[b][0:16, 0:512]))
        v_stageB = M("vector")
        bkS.r_trk = (t_vec, v_stageB)
    CB("vector")
    P("vector", lambda e: e.tensor_copy(out=SHB, in_=SH))
    v_shb = M("vector")

    BAR()
    W("gpsimd", t_pe, t_pe.n)
    v_cwt = DMA("gpsimd", s_cwt, lambda e: e.dma_start(out=CWTB, in_=d_cwt))
    W("tensor", v_cwt)
    W("tensor", t_vec, v_shb)
    v_dscr_read = 0
    for t in range(DT):
        for wi in range(2):
            l0 = wi * 512
            bk = banks[(t * 2 + wi) % 4]
            pe_acquire(bk)
            mm(pb[bk.idx][:, 0:512], CWTB[:, t * 128:(t + 1) * 128],
               SHB[:, l0:l0 + 512], True, True, inc=True)
            bk.w_pe = t_pe.n
            W("scalar", t_vec, v_dscr_read)
            P("scalar", lambda e, t=t, l0=l0: e.activation(
                out=SCR[:, 0:512], in_=XPBj(t, PAD + l0, PAD + l0 + 512),
                func=AF.Copy, scale=DCOL[:, t:t + 1]))
            vsc = M("scalar")
            W("vector", t_pe, bk.w_pe)
            W("vector", t_scl, vsc)
            P("vector", lambda e, t=t, l0=l0, b=bk.idx: e.tensor_add(
                out=YPRE(t, l0, l0 + 512), in0=pb[b][:, 0:512],
                in1=SCR[:, 0:512]))
            vv = M("vector")
            bk.r_trk = (t_vec, vv)
            v_dscr_read = vv
    v_ypre = t_vec.n

    BAR()
    v_wout = {}
    W("gpsimd", t_pe, t_pe.n)
    for t in range(2):
        v_wout[t] = DMA("gpsimd", s_wro[t % 2],
                        lambda e, t=t: e.dma_start(out=WRO(t % 2),
                                                   in_=d_wout[t]))
    W("tensor", t_vec, v_ypre)
    v_yatt_prev = {0: 0, 1: 0}
    for t in range(DT):
        W("tensor", v_wout[t])
        for wi in range(2):
            l0 = wi * 512
            bk = banks[(t * 2 + wi) % 4]
            pe_acquire(bk)
            for j in range(DT):
                mm(pb[bk.idx][:, 0:512], WRO(t % 2)[:, j * 128:(j + 1) * 128],
                   YPRE(j, l0, l0 + 512), j == 0, j == DT - 1, inc=(j == DT - 1))
            bk.w_pe = t_pe.n
            sl = (t * 2 + wi) % 2
            W("scalar", t_pe, bk.w_pe)
            if t * 2 + wi >= 2:
                W("scalar", t_vec, v_yatt_prev[sl])
            P("scalar", lambda e, t=t, b=bk.idx, sl=sl: e.activation(
                out=SQT(sl)[:, 0:512], in_=pb[b][:, 0:512], func=AF.Identity,
                bias=BOUTC[:, t:t + 1]))
            vsc = M("scalar")
            bk.r_trk = (t_scl, vsc)
            W("vector", t_scl, vsc)
            P("vector", lambda e, t=t, l0=l0, sl=sl: e.tensor_add(
                out=YATT(t, l0, l0 + 512), in0=SQT(sl)[:, 0:512],
                in1=XN1j(t, PAD + l0, PAD + l0 + 512)))
            vv = M("vector")
            v_yatt_prev[sl] = vv
        if t + 2 < DT:
            W("gpsimd", t_pe, t_pe.n)
            v_wout[t + 2] = DMA("gpsimd", s_wro[t % 2],
                                lambda e, t=t: e.dma_start(out=WRO(t % 2),
                                                           in_=d_wout[t + 2]))
    v_yatt_all = t_vec.n

    BAR()
    WL2 = [(0, 512), (512, 1024)]

    def outer1_apply(wi, c0, c1, v_chain):
        wl = c1 - c0
        W("vector", t_scl, v_chain)
        for j in range(DT):
            P("vector", lambda e, j=j, c0=c0, wl=wl: e.tensor_sub(
                out=SQT(2)[:, 0:wl], in0=YATT(j, c0, c0 + wl),
                in1=MB_B[:, 0:wl]))
            CB("vector")
            P("vector", lambda e, j=j, c0=c0, wl=wl: e.tensor_mul(
                out=CH2[:, 0:wl], in0=SQT(2)[:, 0:wl], in1=INVB[:, 0:wl]))
            CB("vector")
            P("vector", lambda e, j=j, c0=c0, wl=wl: e.tensor_add(
                out=Xj(j, PAD + c0, PAD + c0 + wl),
                in0=Xj(j, PAD + c0, PAD + c0 + wl), in1=CH2[:, 0:wl]))
            CB("vector")
        ln_state["mb_busy"] = (t_vec, M("vector"))

    emit_ln(WL2, lambda j, c0, c1: YATT(j, c0, c1), False,
            [(None, MB_B, INVB)] * 2, outer1_apply, (t_vec, v_yatt_all))
    v_x2 = ln_state["mb_busy"][1]

    BAR()
    emit_ln(WL2, lambda j, c0, c1: Xj(j, PAD + c0, PAD + c1), True,
            [(MB_F, None, INVB), (MB_F2, None, INVB2)], None, (t_vec, v_x2))
    v_ln2bc = t_scl.n

    act_table("silu_and_others")

    BAR()
    v_wg = {}
    v_wu = {}
    v_wd = {}
    for k in range(2):
        l0 = 512 * k
        mbf = MB_F if k == 0 else MB_F2
        invb = INVB if k == 0 else INVB2
        W("vector", t_scl, v_ln2bc)
        for j in range(DT):
            P("vector", lambda e, j=j, l0=l0, mbf=mbf: e.tensor_sub(
                out=SQT(3)[:, 0:512], in0=Xj(j, PAD + l0, PAD + l0 + 512),
                in1=mbf[:, 0:512]))
            CB("vector")
            P("vector", lambda e, j=j, k=k, invb=invb: e.tensor_mul(
                out=X2NW(k, j), in0=SQT(3)[:, 0:512], in1=invb[:, 0:512]))
            CB("vector")
        v_x2n = M("vector")
        W("gpsimd", t_pe, t_pe.n)
        for i in range(2):
            v_wg[i] = DMA("gpsimd", s_wrf[i % 2],
                          lambda e, i=i: e.dma_start(out=WRS(i % 2),
                                                     in_=d_wg[i]))
            v_wu[i] = DMA("gpsimd", s_wrf[2 + i % 2],
                          lambda e, i=i: e.dma_start(out=WRS(2 + i % 2),
                                                     in_=d_wu[i]))
        W("tensor", t_vec, v_x2n)
        v_fuse_last = 0
        for i in range(IT):
            W("tensor", v_wg[i])
            W("tensor", v_wu[i])
            bg = banks[i % 2]
            bu = banks[2 + i % 2]
            pe_acquire(bg)
            pe_acquire(bu)
            for j in range(DT):
                mm(pb[bg.idx][:, 0:512], WRS(i % 2)[:, j * 128:(j + 1) * 128],
                   X2NW(k, j), j == 0, j == DT - 1, inc=(j == DT - 1))
            bg.w_pe = t_pe.n
            for j in range(DT):
                mm(pb[bu.idx][:, 0:512], WRS(2 + i % 2)[:, j * 128:(j + 1) * 128],
                   X2NW(k, j), j == 0, j == DT - 1, inc=(j == DT - 1))
            bu.w_pe = t_pe.n
            W("scalar", t_pe, bg.w_pe)
            if i >= 1:
                W("scalar", t_vec, v_fuse_last)
            P("scalar", lambda e, i=i, b=bg.idx: e.activation(
                out=GS, in_=pb[b][:, 0:512], func=AF.Silu,
                scale=SGC[:, i:i + 1]))
            vsc = M("scalar")
            bg.r_trk = (t_scl, vsc)
            W("vector", t_scl, vsc)
            W("vector", t_pe, bu.w_pe)
            P("vector", lambda e, i=i, b=bu.idx: e.scalar_tensor_tensor(
                out=FUSE(i), in0=pb[b][:, 0:512], scalar=SUC[:, i:i + 1],
                in1=GS, op0=ALU.mult, op1=ALU.mult))
            v_fuse_last = M("vector")
            bu.r_trk = (t_vec, v_fuse_last)
            if i + 2 < IT:
                W("gpsimd", t_pe, t_pe.n)
                v_wg[i + 2] = DMA("gpsimd", s_wrf[i % 2],
                                  lambda e, i=i: e.dma_start(
                                      out=WRS(i % 2), in_=d_wg[i + 2]))
                v_wu[i + 2] = DMA("gpsimd", s_wrf[2 + i % 2],
                                  lambda e, i=i: e.dma_start(
                                      out=WRS(2 + i % 2), in_=d_wu[i + 2]))
        W("gpsimd", t_pe, t_pe.n)
        for (tt, hh) in [(0, 0), (0, 1)]:
            v_wd[(tt, hh)] = DMA("gpsimd", s_wrd[(tt * 2 + hh) % 2],
                                 lambda e, t=tt, hf=hh: e.dma_start(
                                     out=WRD((t * 2 + hf) % 2),
                                     in_=d_wd[t, hf]))
        W("tensor", t_vec, v_fuse_last)
        for t in range(DT):
            bk = banks[4 + t % 2]
            pe_acquire(bk)
            for hf in range(2):
                W("tensor", v_wd[(t, hf)])
                for kk in range(32):
                    ki = hf * 32 + kk
                    mm(pb[bk.idx][:, 0:512],
                       WRD((t * 2 + hf) % 2)[:, kk * 128:(kk + 1) * 128],
                       FUSE(ki), ki == 0, ki == IT - 1, inc=(ki == IT - 1))
                nt, nh = (t, hf + 1) if hf == 0 else (t + 1, 0)
                nt2, nh2 = (nt, nh + 1) if nh == 0 else (nt + 1, 0)
                pre = (nt2, nh2)
                if pre[0] < DT and pre not in v_wd:
                    W("gpsimd", t_pe, t_pe.n)
                    v_wd[pre] = DMA("gpsimd", s_wrd[(pre[0] * 2 + pre[1]) % 2],
                                    lambda e, p=pre: e.dma_start(
                                        out=WRD((p[0] * 2 + p[1]) % 2),
                                        in_=d_wd[p[0], p[1]]))
            bk.w_pe = t_pe.n
            W("scalar", t_pe, bk.w_pe)
            P("scalar", lambda e, t=t, k=k, b=bk.idx: e.activation(
                out=Y2W(k, t), in_=pb[b][:, 0:512], func=AF.Identity,
                scale=SDC[:, t:t + 1]))
            vsc = M("scalar")
            bk.r_trk = (t_scl, vsc)
        v_wd = {}
    v_y2_all = t_scl.n

    act_table("natural_log_exp_and_others")

    BAR()
    d_out3 = d_out.rearrange("p (j c) -> p j c", j=DT)
    v_out_parts = []

    def final_apply(wi, c0, c1, v_chain):
        wl = c1 - c0
        W("vector", t_scl, v_chain)
        for j in range(DT):
            P("vector", lambda e, j=j, wi=wi, wl=wl: e.tensor_sub(
                out=SQT(2)[:, 0:wl], in0=Y2W(wi, j), in1=MB_B[:, 0:wl]))
            CB("vector")
            P("vector", lambda e, j=j, wl=wl: e.tensor_mul(
                out=CH2[:, 0:wl], in0=SQT(2)[:, 0:wl], in1=INVB[:, 0:wl]))
            CB("vector")
            P("vector", lambda e, j=j, c0=c0, wl=wl: e.tensor_add(
                out=Xj(j, PAD + c0, PAD + c0 + wl),
                in0=Xj(j, PAD + c0, PAD + c0 + wl), in1=CH2[:, 0:wl]))
            CB("vector")
        v_apply = M("vector")
        ln_state["mb_busy"] = (t_vec, v_apply)
        W("sync", t_vec, v_apply)
        v_out_parts.append(DMA("sync", s_out, lambda e, c0=c0, c1=c1:
                               e.dma_start(out=d_out3[:, :, c0:c1],
                                           in_=x_ar3[:, :, PAD + c0:PAD + c1])))

    emit_ln(WL2, lambda j, c0, c1: Y2W(0 if c0 == 0 else 1, j), False,
            [(None, MB_B, INVB)] * 2, final_apply, (t_scl, v_y2_all))

    def tap(name, view, wait_trk, wait_val):
        if name in tap_out:
            W("sync", wait_trk, wait_val)
            DMA("sync", s_tap, lambda e, d=tap_out[name], v=view:
                e.dma_start(out=d, in_=v))

    tap("xn1b", XN1T, t_vec, v_xn1_done)
    tap("x_projb", XPBT, t_scl, v_xpb_all)
    tap("deltab", DB, t_vec, v_dbm)
    tap("S_h", SH, t_vec, v_shb)
    tap("ypre0", YPRET[0], t_vec, v_ypre)
    tap("yatt", XPBT, t_vec, v_yatt_all)
    tap("x2", XT, t_vec, v_x2)
    tap("y2b", Y2AT, t_scl, v_y2_all)

    W("gpsimd", s_out, s_out.n)
    if s_tap.n:
        W("gpsimd", s_tap, s_tap.n)

    with nc.Block() as block:
        @block.sync
        def _(e):
            for f in plan["sync"]:
                f(e)

        @block.gpsimd
        def _(e):
            for f in plan["gpsimd"]:
                f(e)

        @block.scalar
        def _(e):
            for f in plan["scalar"]:
                f(e)

        @block.vector
        def _(e):
            for f in plan["vector"]:
                f(e)

        @block.tensor
        def _(e):
            for f in plan["tensor"]:
                f(e)

    return nc



def _prep_shared(inp):
    f32 = np.float32
    bf = ml_dtypes.bfloat16

    def tile_kblocks(w, nblk):
        K = w.shape[0]
        out = np.empty((nblk, 128, K), w.dtype)
        for t in range(nblk):
            blk = w[:, t * 128:(t + 1) * 128]
            out[t] = blk.reshape(K // 128, 128, 128).transpose(1, 0, 2).reshape(
                128, K)
        return np.ascontiguousarray(out)

    w_in = np.asarray(inp["L0_attn_proj_in_w"], f32)
    w_out = np.asarray(inp["L0_attn_proj_out_w"], f32)
    sh = {}
    sh["w_in_t"] = tile_kblocks(np.ascontiguousarray(w_in.T), DT)
    sh["w_out_t"] = tile_kblocks(np.ascontiguousarray(w_out.T), DT)
    bwt = np.ascontiguousarray(np.asarray(inp["L0_attn_B_w"], f32).T)
    sh["b_w_t"] = np.ascontiguousarray(
        bwt.reshape(16, 128, 16).transpose(1, 0, 2).reshape(128, 256))
    sh["c_w_t"] = np.ascontiguousarray(np.asarray(inp["L0_attn_C_w"], f32).T)
    a = np.asarray(inp["L0_attn_A"], f32)
    sh["a_t"] = np.ascontiguousarray(
        a.reshape(16, 128, 16).transpose(1, 0, 2).reshape(128, 256))
    for nm, key in [("dcol", "L0_attn_D_param"), ("bincol", "L0_attn_proj_in_b"),
                    ("boutcol", "L0_attn_proj_out_b"), ("sdcol", "L0_ffn_s_down")]:
        sh[nm] = np.ascontiguousarray(np.asarray(inp[key], f32).reshape(16, 128).T)
    for nm, key in [("sgcol", "L0_ffn_s_gate"), ("sucol", "L0_ffn_s_up")]:
        sh[nm] = np.ascontiguousarray(np.asarray(inp[key], f32).reshape(64, 128).T)
    sh["wg_t"] = tile_kblocks(np.asarray(inp["L0_ffn_w_gate"]), IT)
    sh["wu_t"] = tile_kblocks(np.asarray(inp["L0_ffn_w_up"]), IT)
    wd = np.asarray(inp["L0_ffn_w_down"])
    wd_t = np.empty((DT, 2, 128, 4096), np.int8)
    for t in range(DT):
        for hf in range(2):
            blk = wd[hf * 4096:(hf + 1) * 4096, t * 128:(t + 1) * 128]
            wd_t[t, hf] = blk.reshape(32, 128, 128).transpose(1, 0, 2).reshape(
                128, 4096)
    sh["wd_t"] = np.ascontiguousarray(wd_t)
    rep = np.zeros((16, 128), bf)
    for c in range(128):
        rep[c % 16, c] = 1.0
    sh["rep16b"] = rep
    sh["ones_b"] = np.ones((128, 1), bf)
    sh["ones_f"] = np.ones((1, 128), f32)
    sh["ood_f"] = np.full((1, 128), 1.0 / D, f32)
    sh["ident_f"] = np.eye(128, dtype=f32)
    return sh


def _prep_core(x, b, h):
    f32 = np.float32
    bf = ml_dtypes.bfloat16
    xt = np.ascontiguousarray(x[b].T.astype(f32))
    if h == 0:
        sl = np.concatenate([np.zeros((D, PAD), f32), xt[:, 0:TOUT]], axis=1)
    else:
        sl = xt[:, TOUT - PAD:2 * TOUT]
    xT = np.ascontiguousarray(
        sl.reshape(16, 128, T).transpose(1, 0, 2).reshape(128, DT * T))
    mask = np.ones((16, T), bf)
    if h == 0:
        mask[:, 0:PAD] = 0
    return {"xT": xT, "maskb": mask}


def kernel(**inputs):
    x = np.asarray(inputs["L0_attn_x"], np.float32)
    sh = _prep_shared(inputs)
    in_maps = []
    for c in range(8):
        b, h = divmod(c, 2)
        m = dict(sh)
        m.update(_prep_core(x, b, h))
        in_maps.append(m)
    nc = build_program()
    res = run_bass_kernel_spmd(nc, in_maps, core_ids=list(range(8)))
    out = np.empty((4, 2048, 2048), np.float32)
    for c in range(8):
        b, h = divmod(c, 2)
        o = np.asarray(res.results[c]["out_xT"])
        ot = o.reshape(128, DT, TOUT).transpose(1, 0, 2).reshape(D, TOUT)
        out[b, h * TOUT:(h + 1) * TOUT, :] = ot.T
    return out
